# version 01 (final)
# baseline (speedup 1.0000x reference)
import numpy as np
import ml_dtypes
from contextlib import ExitStack

import concourse.bass as bass
import concourse.mybir as mybir
from concourse.bass_utils import run_bass_kernel_spmd

F32 = mybir.dt.float32
BF16 = mybir.dt.bfloat16
AF = mybir.ActivationFunctionType
ALU = mybir.AluOpType
AX = mybir.AxisListType

BATCH, SEQ, D = 4, 4096, 2048
TCORE = 2048
W = 512
NBLK = 4
HQ, HKV, HD = 16, 4, 128
CAP = 20.0
SCALE1 = 1.0 / (np.sqrt(128.0) * CAP)
NEG = -5.0e28
EPS = 1e-5
E, F, FS = 8, 1408, 2816
NPAIR = 11
NXE = 10
THALF = 1024

ENGINES = ("sync", "gpsimd", "scalar", "vector", "tensor")
MM, EV, DV = "mm", "ev", "dv"


class Plan:
    def __init__(self):
        self.ops = {e: [] for e in ENGINES}
        self.counts = {s: 0 for s in (MM, EV, DV)}

    def op(self, eng, fn, waits=(), inc=None, n=1):
        ww = []
        for w in waits:
            if w is None:
                continue
            if isinstance(w, list):
                ww.extend(x for x in w if x is not None)
            else:
                ww.append(w)
        self.ops[eng].append((fn, ww, inc, n))
        if inc is not None:
            self.counts[inc] = self.counts.get(inc, 0) + n
            return self.counts[inc]
        return None

    def cnt(self, sem):
        return self.counts.get(sem, 0)

    def stream(self, name):
        return Stream(self, name)


class Stream:

    def __init__(self, plan, name):
        self.plan = plan
        self.name = name
        self.plan.counts.setdefault(name, 0)

    def dma(self, eng, fn, waits=()):
        self.plan.op(eng, fn, waits, inc=self.name, n=16)
        return self.ok()

    def ok(self):
        return (self.name, self.plan.counts.get(self.name, 0))


def _emit(nc, plan, stack):
    sems = {name: stack.enter_context(nc.semaphore(name))
            for name, c in plan.counts.items() if c > 0}
    with nc.Block() as blk:
        for eng_name in ENGINES:
            ops = plan.ops[eng_name]

            def body(e, _ops=ops):
                watermark = {}
                for fn, waits, inc, n in _ops:
                    for s, v in waits:
                        if v and watermark.get(s, 0) < v:
                            e.wait_ge(sems[s], v)
                            watermark[s] = v
                    ins = fn(e)
                    if inc is not None:
                        ins.then_inc(sems[inc], n)

            getattr(blk, eng_name)(body)


def bcast4(ap128):
    return bass.AP(ap128.tensor, ap128.offset,
                   [ap128.ap[0], [0, 4], [1, 128]])


_NC_CACHE = {}


def build_program(debug=False, a_blocks=NBLK, do_b=True, b_experts=NXE,
                  b_halves=2):
    key = (int(debug), a_blocks, do_b, b_experts, b_halves)
    if key in _NC_CACHE:
        return _NC_CACHE[key]

    nc = bass.Bass("TRN2", target_bir_lowering=False)
    P = Plan()
    stack = ExitStack()
    last = {}

    def guard(tag):
        return last.get(tag, [])

    def setg(tag, *pairs):
        last[tag] = [p for p in pairs if p is not None]

    def din(name, shape, dtype=F32):
        return nc.dram_tensor(name, list(shape), dtype, kind="ExternalInput").ap()

    x_own = din("x_own", [TCORE, D])
    x_halo = din("x_halo", [W, D])
    wqT = din("wqT", [D, D])
    wkT = din("wkT", [D, 512])
    wvT = din("wvT", [D, 512])
    woT = din("woT", [D, D])
    gwT = din("gwT", [D, 8])
    guwT = din("guwT", [E, D, 2 * F])
    downT = din("downT", [E, F, D])
    sguT = din("sguT", [D, 2 * FS])
    sdT = din("sdT", [FS, D])
    cos_t = din("cos_t", [128, 2560], BF16)
    sin_t = din("sin_t", [128, 2560], BF16)
    tri_lo = din("tri_lo", [128, 128])
    tri_hi = din("tri_hi", [128, 128])
    halo_b = din("halo_b", [128, 1])
    ident_f = din("ident_f", [128, 128])
    ones81 = din("ones81", [8, 1])
    ones18 = din("ones18", [1, 8])
    sel_e = din("sel_e", [8, 8 * 128], BF16)

    out_hbm = nc.dram_tensor("out", [TCORE, D], F32, kind="ExternalOutput").ap()
    x1_hbm = nc.dram_tensor("x1_scratch", [TCORE, D], F32, kind="Internal").ap()
    dbg = {}
    if debug:
        dbg["x1"] = nc.dram_tensor("dbg_x1", [TCORE, D], F32,
                                   kind="ExternalOutput").ap()
        dbg["comb"] = nc.dram_tensor("dbg_comb", [8, TCORE], BF16,
                                     kind="ExternalOutput").ap()
    if debug and debug >= 3:
        for nm, shp in [("dbg_k0", [128, 2048]), ("dbg_k1", [128, 2048]),
                        ("dbg_v0", [128, 16 * 132]), ("dbg_v1", [128, 16 * 132]),
                        ("dbg_qend", [128, 16 * 512]),
                        ("dbg_xn2", [128, 16 * THALF]),
                        ("dbg_y", [128, 8 * 2048]),
                        ("dbg_act", [128, 11 * 1024]),
                        ("dbg_cb", [128, 1024])]:
            dbg[nm] = nc.dram_tensor(nm, shp, BF16, kind="ExternalOutput").ap()

    def sb(st_, name, shape, dtype):
        return st_.enter_context(nc.sbuf_tensor(name, list(shape), dtype))

    def psal(st_, name, shape, dtype):
        return st_.enter_context(nc.psum_tensor(name, list(shape), dtype))

    c_ident = sb(stack, "c_ident", [128, 128], BF16)
    c_trilo = sb(stack, "c_trilo", [128, 128], F32)
    c_trihi = sb(stack, "c_trihi", [128, 128], F32)
    c_halob = sb(stack, "c_halob", [128, 1], F32)
    c_ones81 = sb(stack, "c_ones81", [8, 1], F32)
    c_ones18 = sb(stack, "c_ones18", [1, 8], F32)
    c_sel = sb(stack, "c_sel", [8, 8 * 128], BF16)
    c_eps = sb(stack, "c_eps", [128, 1], F32)

    s_cst = P.stream("cst")
    for dst, src in [(c_trilo, tri_lo), (c_trihi, tri_hi), (c_halob, halo_b),
                     (c_ones81, ones81), (c_ones18, ones18), (c_sel, sel_e)]:
        s_cst.dma("sync", lambda e, dst=dst, src=src: e.dma_start(
            out=dst[:], in_=src))
    s_idl = P.stream("idl")
    s_idl.dma("gpsimd", lambda e: e.dma_start(out=c_ident[:], in_=ident_f))
    P.op("vector", lambda e: e.memset(c_eps[:], EPS), inc=DV)

    a_stack = ExitStack()
    wo_sb = sb(a_stack, "wo_sb", [128, 16 * 2048], BF16)
    wqk = [sb(a_stack, f"wqk{i}", [128, 16 * 128], BF16) for i in range(4)]
    wvp = [sb(a_stack, f"wvp{i}", [128, 512], BF16) for i in range(4)]
    cs_b = [sb(a_stack, f"cs_b{i}", [128, 512], BF16) for i in range(2)]
    sn_b = [sb(a_stack, f"sn_b{i}", [128, 512], BF16) for i in range(2)]
    xch = [sb(a_stack, f"xch{i}", [128, 2048], F32) for i in range(2)]
    xnT_b = sb(a_stack, "xnT_b", [128, 16 * 512], BF16)
    out2_blk = xnT_b
    xn_bf = sb(a_stack, "xn_bf", [128, 2048], BF16)
    q_b = sb(a_stack, "q_b", [128, 16 * 512], BF16)
    qk_raw = sb(a_stack, "qk_raw", [128, 512], BF16)
    qk_sw = sb(a_stack, "qk_sw", [128, 512], BF16)
    k_ring = [sb(a_stack, f"k_ring{p}", [128, 4 * 512], BF16) for p in range(2)]
    v_ring = [sb(a_stack, f"v_ring{p}", [128, 16 * 132], BF16) for p in range(2)]
    pT = [sb(a_stack, f"pT{p}", [128, 5 * 512], BF16) for p in range(2)]
    sc_buf = [sb(a_stack, f"sc_buf{p}", [128, 512], F32) for p in range(2)]
    ao_m = sb(a_stack, "ao_m", [128, 2048], BF16)
    aoT_m = ao_m
    rb_sc = sb(a_stack, "rb_sc", [128, 4], F32)
    st_sum = sb(a_stack, "st_sum", [128, 16], F32)
    st_sq = sb(a_stack, "st_sq", [128, 16], F32)
    st_s1 = sb(a_stack, "st_s1", [128, 16], F32)
    st_rs = sb(a_stack, "st_rs", [128, 16], F32)
    ln_s1 = sb(a_stack, "ln_s1", [128, 4], F32)
    ln_s2 = sb(a_stack, "ln_s2", [128, 4], F32)
    sq_scr = sb(a_stack, "sq_scr", [128, 2048], F32)
    tmp_s = sb(a_stack, "tmp_s", [128, 32], F32)
    u_buf = sb(a_stack, "u_buf", [128, 512], BF16)
    w_buf = sb(a_stack, "w_buf", [128, 512], BF16)

    ps_tp = psal(a_stack, "ps_tp", [128, 2048], BF16)
    ps_S = [psal(a_stack, f"ps_S{p}", [128, 512], F32) for p in range(2)]
    ps_O = [psal(a_stack, f"ps_O{p}", [128, 264], F32) for p in range(2)]
    ps_P = [psal(a_stack, f"ps_P{p}", [128, 512], F32) for p in range(2)]

    s_wol = P.stream("wol")
    for kc in range(16):
        s_wol.dma("gpsimd", lambda e, kc=kc: e.dma_start(
            out=wo_sb[:, kc * 2048:(kc + 1) * 2048],
            in_=woT[kc * 128:(kc + 1) * 128, :]))
    s_wq = [P.stream(f"wq{i}") for i in range(4)]
    s_wv = [P.stream(f"wv{i}") for i in range(4)]
    s_cs = [P.stream(f"cs{i}") for i in range(2)]
    s_xc = [P.stream(f"xc{i}") for i in range(2)]
    s_swp = P.stream("swp")
    s_st1 = [P.stream(f"st1{i}") for i in range(2)]
    s_dbg = P.stream("dbgS")

    vones = None
    for p in range(2):
        vones = P.op("vector", lambda e, p=p: e.memset(
            v_ring[p][:].rearrange("p (s c) -> p s c", c=132)[:, :, 128:129],
            1.0), inc=DV)
    vones_dv = (DV, vones)

    def load_pool(idx, col0, src):
        return s_wq[idx].dma("gpsimd", lambda e, idx=idx, col0=col0, src=src:
                             e.dma_start(
            out=wqk[idx][:].rearrange("p (kc f) -> p kc f", kc=16),
            in_=src[:, col0:col0 + 128].rearrange("(kc p) f -> p kc f", p=128)),
            waits=[guard(("wqk", idx))])

    def load_cs(par, col0):
        s_cs[par].dma("sync", lambda e, par=par, col0=col0: e.dma_start(
            out=cs_b[par][:], in_=cos_t[:, col0:col0 + 512]),
            waits=[guard(("cs", par))])
        return s_cs[par].dma("sync", lambda e, par=par, col0=col0: e.dma_start(
            out=sn_b[par][:], in_=sin_t[:, col0:col0 + 512]))

    def emit_ln1_transpose(tok0, src):
        lastev = None
        for tc in range(4):
            buf = xch[tc % 2]
            s = src[tok0 + tc * 128: tok0 + (tc + 1) * 128, :]
            xl = s_xc[tc % 2].dma("sync", lambda e, buf=buf, s=s: e.dma_start(
                out=buf[:], in_=s), waits=[guard(("xch", tc % 2))])
            dv = P.op("vector", lambda e, buf=buf, tc=tc: e.reduce_sum(
                out=ln_s1[:, tc:tc + 1], in_=buf[:], axis=AX.X),
                waits=[xl], inc=DV)
            ev = P.op("scalar", lambda e, buf=buf, tc=tc: e.activation(
                out=sq_scr[:], in_=buf[:], func=AF.Square,
                accum_out=ln_s2[:, tc:tc + 1]),
                waits=[xl, guard("sq_scr")], inc=EV)
            setg("sq_scr", (EV, ev))
            dv = P.op("vector", lambda e, tc=tc: e.tensor_scalar(
                out=st_s1[:, 8 + tc:9 + tc], in0=ln_s1[:, tc:tc + 1],
                scalar1=1.0 / 2048.0, scalar2=None, op0=ALU.mult),
                waits=[(DV, dv), (EV, ev)], inc=DV)
            dv = P.op("vector", lambda e, tc=tc: e.scalar_tensor_tensor(
                out=tmp_s[:, tc:tc + 1], in0=st_s1[:, 8 + tc:9 + tc],
                scalar=-1.0, in1=st_s1[:, 8 + tc:9 + tc],
                op0=ALU.mult, op1=ALU.mult), waits=[(DV, dv)], inc=DV)
            dv = P.op("vector", lambda e, tc=tc: e.scalar_tensor_tensor(
                out=tmp_s[:, 4 + tc:5 + tc], in0=ln_s2[:, tc:tc + 1],
                scalar=1.0 / 2048.0, in1=tmp_s[:, tc:tc + 1],
                op0=ALU.mult, op1=ALU.add), waits=[(DV, dv)], inc=DV)
            ev = P.op("scalar", lambda e, tc=tc: e.activation(
                out=tmp_s[:, 8 + tc:9 + tc], in_=tmp_s[:, 4 + tc:5 + tc],
                func=AF.Ln, bias=c_eps[:, 0:1]), waits=[(DV, dv)], inc=EV)
            ev = P.op("scalar", lambda e, tc=tc: e.activation(
                out=st_rs[:, 8 + tc:9 + tc], in_=tmp_s[:, 8 + tc:9 + tc],
                func=AF.Exp, scale=-0.5), waits=[(EV, ev)], inc=EV)
            dv = P.op("vector", lambda e, buf=buf, tc=tc: e.tensor_scalar(
                out=xn_bf[:], in0=buf[:], scalar1=st_s1[:, 8 + tc:9 + tc],
                scalar2=st_rs[:, 8 + tc:9 + tc], op0=ALU.subtract,
                op1=ALU.mult), waits=[(EV, ev), guard("xn_bf")], inc=DV)
            setg(("xch", tc % 2), (DV, dv))
            for kc in range(16):
                mm = P.op("tensor", lambda e, kc=kc: e.transpose(
                    ps_tp[:, kc * 128:(kc + 1) * 128],
                    xn_bf[:, kc * 128:(kc + 1) * 128], c_ident[:]),
                    waits=([(DV, dv), s_idl.ok(), guard("ps_tp")]
                           if kc == 0 else []),
                    inc=(MM if kc == 15 else None))
            setg("xn_bf", (MM, mm))
            ev = P.op("scalar", lambda e, tc=tc: e.copy(
                out=xnT_b[:].rearrange("p (kc t) -> p kc t", t=512)
                [:, :, tc * 128:(tc + 1) * 128],
                in_=ps_tp[:].rearrange("p (kc t) -> p kc t", t=128)),
                waits=[(MM, mm)], inc=EV)
            setg("ps_tp", (EV, ev))
            lastev = ev
        return (EV, lastev)

    def rope_to(dst, dst_off, cpar, pp, mmw, cs_ok):
        ev = P.op("scalar", lambda e, pp=pp: e.copy(
            out=qk_raw[:], in_=ps_P[pp][:]),
            waits=[(MM, mmw), guard("qk_raw")], inc=EV)
        setg(("ps_P", pp), (EV, ev))
        s_swp.dma("sync", lambda e: e.dma_start(
            out=qk_sw[0:64, :], in_=qk_raw[64:128, :]),
            waits=[(EV, ev), guard("qk_sw")])
        xl = s_swp.dma("sync", lambda e: e.dma_start(
            out=qk_sw[64:128, :], in_=qk_raw[0:64, :]))
        dv = P.op("vector", lambda e, cpar=cpar: e.tensor_mul(
            u_buf[:], qk_raw[:], cs_b[cpar][:]),
            waits=[(EV, ev), cs_ok, guard("u_buf")], inc=DV)
        setg("qk_raw", (DV, dv), xl)
        dv = P.op("vector", lambda e, cpar=cpar: e.tensor_mul(
            w_buf[:], qk_sw[:], sn_b[cpar][:]),
            waits=[xl, guard("w_buf")], inc=DV)
        setg("qk_sw", (DV, dv))
        dv = P.op("vector", lambda e, dst=dst, o=dst_off: e.tensor_add(
            dst[:, o:o + 512], u_buf[:], w_buf[:]), inc=DV)
        setg("u_buf", (DV, dv))
        setg("w_buf", (DV, dv))
        return dv

    def emit_kv(parity, cpar, cs_ok, xnT_ev):
        kdv = None
        for g in range(4):
            pp = g % 2
            wl = load_pool(g % 4, g * 128, wkT)
            for kc in range(16):
                mm = P.op("tensor", lambda e, kc=kc, g=g, pp=pp: e.matmul(
                    ps_P[pp][:],
                    wqk[g % 4][:, kc * 128:(kc + 1) * 128],
                    xnT_b[:, kc * 512:(kc + 1) * 512],
                    start=(kc == 0), stop=(kc == 15)),
                    waits=([xnT_ev, wl, guard(("ps_P", pp))]
                           if kc == 0 else []),
                    inc=(MM if kc == 15 else None))
            setg(("wqk", g % 4), (MM, mm))
            kdv = rope_to(k_ring[parity], g * 512, cpar, pp, mm, cs_ok)
        vps = [ps_S[0], ps_S[1], ps_P[0], ps_P[1]]
        vtag = [("ps_S", 0), ("ps_S", 1), ("ps_P", 0), ("ps_P", 1)]
        for kc in range(16):
            vb = wvp[kc % 4]
            wl = s_wv[kc % 4].dma("gpsimd", lambda e, kc=kc, vb=vb:
                                  e.dma_start(
                out=vb[:], in_=wvT[kc * 128:(kc + 1) * 128, :]),
                waits=[guard(("wvp", kc % 4))])
            for tc in range(4):
                mm = P.op("tensor", lambda e, kc=kc, tc=tc, vb=vb: e.matmul(
                    vps[tc][:],
                    xnT_b[:, kc * 512 + tc * 128: kc * 512 + (tc + 1) * 128],
                    vb[:], start=(kc == 0), stop=(kc == 15)),
                    waits=([xnT_ev, wl, guard(vtag[tc])]
                           if kc == 0 else ([wl] if tc == 0 else [])),
                    inc=(MM if tc == 3 else None))
            setg(("wvp", kc % 4), (MM, mm))
        vmm = mm
        vev = None
        for tc in range(4):
            vev = P.op("scalar", lambda e, tc=tc, parity=parity: e.copy(
                out=v_ring[parity][:].rearrange("p (s c) -> p s c", c=132)
                [:, tc * 4:(tc + 1) * 4, 0:128],
                in_=vps[tc][:].rearrange("p (g d) -> p g d", d=128)),
                waits=[(MM, vmm), vones_dv], inc=EV)
            setg(vtag[tc], (EV, vev))
        return (DV, kdv), (EV, vev)

    def emit_q(cpar, cs_ok, xnT_ev):
        dv = None
        for h in range(16):
            pp = h % 2
            wl = load_pool(h % 4, h * 128, wqT)
            for kc in range(16):
                mm = P.op("tensor", lambda e, kc=kc, h=h, pp=pp: e.matmul(
                    ps_P[pp][:],
                    wqk[h % 4][:, kc * 128:(kc + 1) * 128],
                    xnT_b[:, kc * 512:(kc + 1) * 512],
                    start=(kc == 0), stop=(kc == 15)),
                    waits=([xnT_ev, wl, guard(("ps_P", pp))]
                           if kc == 0 else []),
                    inc=(MM if kc == 15 else None))
            setg(("wqk", h % 4), (MM, mm))
            dv = rope_to(q_b, h * 512, cpar, pp, mm, cs_ok)
        return (DV, dv)

    def emit_attn(m, g, kv_ok, q_ok, pT_par):
        ptb = pT[pT_par]
        exp_ev = None
        for r in range(5):
            kcid = m + r - 4
            par = (kcid // 4) % 2
            slot = kcid % 4
            sp = r % 2
            for hh in range(4):
                mm = P.op("tensor", lambda e, r=r, hh=hh, g=g, par=par,
                          slot=slot, sp=sp, m=m: e.matmul(
                    ps_S[sp][:, hh * 128:(hh + 1) * 128],
                    k_ring[par][:, g * 512 + slot * 128:
                                g * 512 + (slot + 1) * 128],
                    q_b[:, (4 * g + hh) * 512 + (m % 4) * 128:
                        (4 * g + hh) * 512 + (m % 4 + 1) * 128],
                    start=True, stop=True),
                    waits=([q_ok, kv_ok[("k", par)], guard(("ps_S", sp))]
                           if hh == 0 else []),
                    inc=(MM if hh == 3 else None))
            ev = P.op("scalar", lambda e, sp=sp: e.activation(
                out=sc_buf[sp][:], in_=ps_S[sp][:], func=AF.Tanh,
                scale=SCALE1), waits=[(MM, mm), guard(("sc", sp))], inc=EV)
            setg(("ps_S", sp), (EV, ev))
            need = [(EV, ev)]
            if r == 0:
                d = P.op("vector", lambda e, sp=sp: e.tensor_add(
                    sc_buf[sp][:], sc_buf[sp][:], bcast4(c_trilo[:])),
                    waits=need + [s_cst.ok()], inc=DV)
                need = [(DV, d)]
            elif r == 4:
                d = P.op("vector", lambda e, sp=sp: e.tensor_add(
                    sc_buf[sp][:], sc_buf[sp][:], bcast4(c_trihi[:])),
                    waits=need + [s_cst.ok()], inc=DV)
                need = [(DV, d)]
            if kcid < 0:
                d = P.op("vector", lambda e, sp=sp: e.tensor_scalar(
                    out=sc_buf[sp][:], in0=sc_buf[sp][:],
                    scalar1=c_halob[:, 0:1], scalar2=None, op0=ALU.add),
                    waits=need + [s_cst.ok()], inc=DV)
                need = [(DV, d)]
            exp_ev = P.op("scalar", lambda e, r=r, sp=sp, ptb=ptb: e.activation(
                out=ptb[:, r * 512:(r + 1) * 512], in_=sc_buf[sp][:],
                func=AF.Exp, scale=CAP),
                waits=need + [guard(("pT", pT_par))], inc=EV)
            setg(("sc", sp), (EV, exp_ev))
        pv_mm = {}
        for hh in range(4):
            po = ps_O[hh // 2]
            off = (hh % 2) * 132
            for r in range(5):
                kcid = m + r - 4
                par = (kcid // 4) % 2
                vsl = ((kcid % 4) * 4 + g) * 132
                mm = P.op("tensor", lambda e, r=r, po=po, off=off, par=par,
                          vsl=vsl, ptb=ptb, hh=hh: e.matmul(
                    po[:, off:off + 129],
                    ptb[:, r * 512 + hh * 128: r * 512 + (hh + 1) * 128],
                    v_ring[par][:, vsl:vsl + 129],
                    start=(r == 0), stop=(r == 4)),
                    waits=([(EV, exp_ev), kv_ok[("v", 0)], kv_ok[("v", 1)],
                            guard(("ps_O", hh // 2))] if r == 0 else []),
                    inc=(MM if r == 4 else None))
            pv_mm[hh] = mm
        setg(("pT", pT_par), (MM, pv_mm[3]))
        lastdv = None
        for hh in range(4):
            po = ps_O[hh // 2]
            off = (hh % 2) * 132
            d = P.op("vector", lambda e, po=po, off=off, hh=hh: e.reciprocal(
                out=rb_sc[:, hh:hh + 1], in_=po[:, off + 128:off + 129]),
                waits=[(MM, pv_mm[3])], inc=DV)
            d = P.op("vector", lambda e, po=po, off=off, hh=hh, g=g:
                     e.tensor_scalar(
                out=ao_m[:, (4 * g + hh) * 128:(4 * g + hh + 1) * 128],
                in0=po[:, off:off + 128], scalar1=rb_sc[:, hh:hh + 1],
                scalar2=None, op0=ALU.mult), waits=[(DV, d)], inc=DV)
            if hh % 2 == 1:
                setg(("ps_O", hh // 2), (DV, d))
            lastdv = d
        return (DV, lastdv)

    def emit_wo(m, ao_ok):
        for kc in range(16):
            mm = P.op("tensor", lambda e, kc=kc: e.transpose(
                ps_tp[:, kc * 128:(kc + 1) * 128],
                ao_m[:, kc * 128:(kc + 1) * 128], c_ident[:]),
                waits=([ao_ok, s_wol.ok(), guard("ps_tp")] if kc == 0 else []),
                inc=(MM if kc == 15 else None))
        ev = P.op("scalar", lambda e: e.copy(out=aoT_m[:], in_=ps_tp[:]),
                  waits=[(MM, mm)], inc=EV)
        setg("ps_tp", (EV, ev))
        outg = []
        for oc in range(4):
            pp = oc % 2
            for kc in range(16):
                mm = P.op("tensor", lambda e, kc=kc, oc=oc, pp=pp: e.matmul(
                    ps_P[pp][:], aoT_m[:, kc * 128:(kc + 1) * 128],
                    wo_sb[:, kc * 2048 + oc * 512: kc * 2048 + (oc + 1) * 512],
                    start=(kc == 0), stop=(kc == 15)),
                    waits=([(EV, ev), guard(("ps_P", pp))] if kc == 0 else []),
                    inc=(MM if kc == 15 else None))
            ev1 = P.op("scalar", lambda e, m=m, oc=oc, pp=pp: e.activation(
                out=out2_blk[:, (m % 4) * 2048 + oc * 512:
                             (m % 4) * 2048 + (oc + 1) * 512],
                in_=ps_P[pp][:], func=AF.Copy,
                accum_out=st_sum[:, (m % 4) * 4 + oc:(m % 4) * 4 + oc + 1]),
                waits=[(MM, mm), guard("st_sum")], inc=EV)
            ev2 = P.op("scalar", lambda e, m=m, oc=oc, pp=pp: e.activation(
                out=sq_scr[:, 0:512], in_=ps_P[pp][:], func=AF.Square,
                accum_out=st_sq[:, (m % 4) * 4 + oc:(m % 4) * 4 + oc + 1]),
                waits=[(MM, mm), guard("sq_scr"), guard("st_sq")], inc=EV)
            setg("sq_scr", (EV, ev2))
            setg(("ps_P", pp), (EV, ev2))
            outg = [(EV, ev2)]
        return outg

    def emit_epilogue(b, marks):
        wts = []
        for g_ in marks:
            wts.extend(g_)
        dv = P.op("vector", lambda e: e.reduce_sum(
            out=st_s1[:, 0:4],
            in_=st_sum[:].rearrange("p (m o) -> p m o", m=4), axis=AX.X),
            waits=wts, inc=DV)
        setg("st_sum", (DV, dv))
        dv = P.op("vector", lambda e: e.reduce_sum(
            out=tmp_s[:, 16:20],
            in_=st_sq[:].rearrange("p (m o) -> p m o", m=4), axis=AX.X),
            waits=[(DV, dv)], inc=DV)
        setg("st_sq", (DV, dv))
        dv = P.op("vector", lambda e: e.tensor_scalar(
            out=st_s1[:, 0:4], in0=st_s1[:, 0:4], scalar1=1.0 / 2048.0,
            scalar2=None, op0=ALU.mult), waits=[(DV, dv)], inc=DV)
        dv = P.op("vector", lambda e: e.scalar_tensor_tensor(
            out=tmp_s[:, 20:24], in0=st_s1[:, 0:4], scalar=-1.0,
            in1=st_s1[:, 0:4], op0=ALU.mult, op1=ALU.mult),
            waits=[(DV, dv)], inc=DV)
        dv = P.op("vector", lambda e: e.scalar_tensor_tensor(
            out=tmp_s[:, 16:20], in0=tmp_s[:, 16:20], scalar=1.0 / 2048.0,
            in1=tmp_s[:, 20:24], op0=ALU.mult, op1=ALU.add),
            waits=[(DV, dv)], inc=DV)
        ev = P.op("scalar", lambda e: e.activation(
            out=tmp_s[:, 24:28], in_=tmp_s[:, 16:20], func=AF.Ln,
            bias=c_eps[:, 0:1]), waits=[(DV, dv)], inc=EV)
        ev = P.op("scalar", lambda e: e.activation(
            out=st_rs[:, 0:4], in_=tmp_s[:, 24:28], func=AF.Exp, scale=-0.5),
            waits=[(EV, ev)], inc=EV)
        for mq in range(4):
            tok0 = b * 512 + mq * 128
            buf = xch[mq % 2]
            xl = s_xc[mq % 2].dma("sync", lambda e, buf=buf, tok0=tok0:
                                  e.dma_start(
                out=buf[:], in_=x_own[tok0:tok0 + 128, :]),
                waits=[guard(("xch", mq % 2))])
            dv = P.op("vector", lambda e, mq=mq: e.tensor_scalar(
                out=sq_scr[:], in0=out2_blk[:, mq * 2048:(mq + 1) * 2048],
                scalar1=st_s1[:, mq:mq + 1], scalar2=st_rs[:, mq:mq + 1],
                op0=ALU.subtract, op1=ALU.mult),
                waits=[(EV, ev), guard("sq_scr")], inc=DV)
            dv = P.op("vector", lambda e, mq=mq, buf=buf: e.tensor_add(
                buf[:], sq_scr[:], buf[:]), waits=[(DV, dv), xl], inc=DV)
            setg("sq_scr", (DV, dv))
            st = s_st1[mq % 2].dma("sync", lambda e, buf=buf, tok0=tok0:
                                   e.dma_start(
                out=x1_hbm[tok0:tok0 + 128, :], in_=buf[:]),
                waits=[(DV, dv)])
            if debug:
                st = s_st1[mq % 2].dma("sync", lambda e, buf=buf, tok0=tok0:
                                       e.dma_start(
                    out=dbg["x1"][tok0:tok0 + 128, :], in_=buf[:]))
            setg(("xch", mq % 2), st)

    kv_ok = {}
    cs_ok = load_cs(1, 0)
    ev = emit_ln1_transpose(0, x_halo)
    kdv, vdv = emit_kv(1, 1, cs_ok, ev)
    kv_ok[("k", 0)] = kv_ok[("k", 1)] = kdv
    kv_ok[("v", 0)] = kv_ok[("v", 1)] = vdv
    setg(("cs", 1), kdv)
    for b in range(a_blocks):
        par = b % 2
        cs_ok = load_cs(par, 512 + b * 512)
        ev = emit_ln1_transpose(b * 512, x_own)
        kdv, vdv = emit_kv(par, par, cs_ok, ev)
        kv_ok[("k", par)] = kdv
        kv_ok[("v", par)] = vdv
        q_ok = emit_q(par, cs_ok, ev)
        marks = []
        for mq in range(4):
            m = b * 4 + mq
            ao_ok = None
            for g in range(4):
                pT_par = (m * 4 + g) % 2
                ao_ok = emit_attn(m, g, kv_ok, q_ok, pT_par)
            marks.append(emit_wo(m, ao_ok))
        emit_epilogue(b, marks)
        setg(("cs", par), (DV, P.cnt(DV)))

    if debug and debug >= 3:
        wend = [(MM, P.cnt(MM)), (EV, P.cnt(EV)), (DV, P.cnt(DV))]
        for nm, src in [("dbg_k0", k_ring[0]), ("dbg_k1", k_ring[1]),
                        ("dbg_v0", v_ring[0]), ("dbg_v1", v_ring[1]),
                        ("dbg_qend", q_b)]:
            s_dbg.dma("sync", lambda e, nm=nm, src=src: e.dma_start(
                out=dbg[nm], in_=src[:]), waits=wend)

    barrier = [(s_, c) for s_, c in P.counts.items() if c > 0]
    a_stack.close()

    b_stack = ExitStack()
    xn2T = sb(b_stack, "xn2T", [128, 16 * THALF], BF16)
    y_half = sb(b_stack, "y_half", [128, 8 * 2048], BF16)
    x1b = [sb(b_stack, f"x1b{i}", [128, 2048], F32) for i in range(2)]
    gw_sb = sb(b_stack, "gw_sb", [128, 128], BF16)
    p8T = sb(b_stack, "p8T", [8, THALF], F32)
    p8nT = sb(b_stack, "p8nT", [8, THALF], BF16)
    g_sh = sb(b_stack, "g_sh", [4, 512], F32)
    g_m1 = sb(b_stack, "g_m1", [4, 512], F32)
    g_keep = sb(b_stack, "g_keep", [8, THALF], F32)
    xn_bf2 = sb(b_stack, "xn_bf2", [128, 2048], BF16)
    sq2 = sb(b_stack, "sq2", [128, 32], F32)
    lnst = sb(b_stack, "lnst", [128, 32], F32)
    gub = [sb(b_stack, f"gub{i}", [128, 16 * 128], BF16) for i in range(4)]
    dwn_sb = sb(b_stack, "dwn_sb", [128, 11 * 2048], BF16)
    act_h = sb(b_stack, "act_h", [128, 11 * 1024], BF16)
    cb_sb = sb(b_stack, "cb_sb", [128, 1024], BF16)
    sg_f = sb(b_stack, "sg_f", [128, 1024], F32)
    ysc = [sb(b_stack, f"ysc{i}", [128, 2048], BF16) for i in range(2)]
    dum_b = ysc[0]
    o_f = sb(b_stack, "o_f", [128, 2048], F32)

    b1_ps = ExitStack()
    ps_tp2 = psal(b1_ps, "ps_tp2", [128, 2048], BF16)
    ps_lg = psal(b1_ps, "ps_lg", [8, THALF], F32)
    ps_s1g = psal(b1_ps, "ps_s1g", [1, THALF], F32)
    ps_b8 = psal(b1_ps, "ps_b8", [8, THALF], F32)
    b1_ps.close()
    b2_ps = ExitStack()
    ps_g = psal(b2_ps, "ps_g", [128, 1024], F32)
    ps_u = psal(b2_ps, "ps_u", [128, 1024], F32)
    ps_y = psal(b2_ps, "ps_y", [128, 2048], F32)
    b2_ps.close()

    s_gwl = P.stream("gwl")
    s_gu = [P.stream(f"gu{i}") for i in range(4)]
    s_dnl = P.stream("dnl")
    s_xb = [P.stream(f"xb{i}") for i in range(2)]
    s_tre = P.stream("tre")
    s_ya = [P.stream(f"ya{i}") for i in range(8)]
    s_ost = P.stream("ost")

    if do_b:
        for kc in range(16):
            s_gwl.dma("gpsimd", lambda e, kc=kc: e.dma_start(
                out=gw_sb[:, kc * 8:(kc + 1) * 8],
                in_=gwT[kc * 128:(kc + 1) * 128, :]),
                waits=(barrier if kc == 0 else []))

    for half in range(b_halves if do_b else 0):
        ht0 = half * THALF

        sdv = sev = None
        for tc in range(8):
            tok0 = ht0 + tc * 128
            buf = x1b[tc % 2]
            w0 = [guard(("x1b", tc % 2))]
            if half == 0 and tc == 0:
                w0 += barrier + [s_st1[0].ok(), s_st1[1].ok()]
            xl = s_xb[tc % 2].dma("sync", lambda e, buf=buf, tok0=tok0:
                                  e.dma_start(
                out=buf[:], in_=x1_hbm[tok0:tok0 + 128, :]), waits=w0)
            sdv = P.op("vector", lambda e, buf=buf, tc=tc: e.reduce_sum(
                out=lnst[:, tc:tc + 1], in_=buf[:], axis=AX.X),
                waits=[xl, guard("lnst")], inc=DV)
            sev = P.op("scalar", lambda e, buf=buf, tc=tc: e.activation(
                out=dum_b[:], in_=buf[:], func=AF.Square,
                accum_out=lnst[:, 8 + tc:9 + tc]),
                waits=[xl, guard(("ysc", 0))], inc=EV)
            setg(("ysc", 0), (EV, sev))
            setg(("x1b", tc % 2), (DV, sdv), (EV, sev))
        dv = P.op("vector", lambda e: e.tensor_scalar(
            out=lnst[:, 16:24], in0=lnst[:, 0:8], scalar1=1.0 / 2048.0,
            scalar2=None, op0=ALU.mult),
            waits=[(DV, sdv), (EV, sev)], inc=DV)
        dv = P.op("vector", lambda e: e.scalar_tensor_tensor(
            out=sq2[:, 0:8], in0=lnst[:, 16:24], scalar=-1.0,
            in1=lnst[:, 16:24], op0=ALU.mult, op1=ALU.mult),
            waits=[(DV, dv), guard("sq2")], inc=DV)
        dv = P.op("vector", lambda e: e.scalar_tensor_tensor(
            out=sq2[:, 8:16], in0=lnst[:, 8:16], scalar=1.0 / 2048.0,
            in1=sq2[:, 0:8], op0=ALU.mult, op1=ALU.add),
            waits=[(DV, dv)], inc=DV)
        ev = P.op("scalar", lambda e: e.activation(
            out=sq2[:, 16:24], in_=sq2[:, 8:16], func=AF.Ln,
            bias=c_eps[:, 0:1]), waits=[(DV, dv)], inc=EV)
        ev = P.op("scalar", lambda e: e.activation(
            out=lnst[:, 24:32], in_=sq2[:, 16:24], func=AF.Exp, scale=-0.5),
            waits=[(EV, ev)], inc=EV)
        rstd = (EV, ev)
        setg("sq2", rstd)
        xn2T_ev = None
        for tc in range(8):
            tok0 = ht0 + tc * 128
            buf = x1b[tc % 2]
            xl = s_xb[tc % 2].dma("sync", lambda e, buf=buf, tok0=tok0:
                                  e.dma_start(
                out=buf[:], in_=x1_hbm[tok0:tok0 + 128, :]),
                waits=[guard(("x1b", tc % 2))])
            dv = P.op("vector", lambda e, buf=buf, tc=tc: e.tensor_scalar(
                out=xn_bf2[:], in0=buf[:],
                scalar1=lnst[:, 16 + tc:17 + tc],
                scalar2=lnst[:, 24 + tc:25 + tc],
                op0=ALU.subtract, op1=ALU.mult),
                waits=[xl, rstd, guard("xn_bf2")], inc=DV)
            setg(("x1b", tc % 2), (DV, dv))
            for kc in range(16):
                mm = P.op("tensor", lambda e, kc=kc: e.transpose(
                    ps_tp2[:, kc * 128:(kc + 1) * 128],
                    xn_bf2[:, kc * 128:(kc + 1) * 128], c_ident[:]),
                    waits=([(DV, dv), guard("ps_tp2")] if kc == 0 else []),
                    inc=(MM if kc == 15 else None))
            setg("xn_bf2", (MM, mm))
            ev = P.op("scalar", lambda e, tc=tc: e.copy(
                out=xn2T[:].rearrange("p (kc t) -> p kc t", t=THALF)
                [:, :, tc * 128:(tc + 1) * 128],
                in_=ps_tp2[:].rearrange("p (kc t) -> p kc t", t=128)),
                waits=[(MM, mm)], inc=EV)
            setg("ps_tp2", (EV, ev))
            xn2T_ev = ev

        for tc2 in range(2):
            for kc in range(16):
                mm = P.op("tensor", lambda e, kc=kc, tc2=tc2: e.matmul(
                    ps_lg[:, tc2 * 512:(tc2 + 1) * 512],
                    gw_sb[:, kc * 8:(kc + 1) * 8],
                    xn2T[:, kc * THALF + tc2 * 512:
                         kc * THALF + (tc2 + 1) * 512],
                    start=(kc == 0), stop=(kc == 15)),
                    waits=([(EV, xn2T_ev), s_gwl.ok(), guard("ps_lg")]
                           if (kc == 0 and tc2 == 0) else []),
                    inc=(MM if kc == 15 else None))
        ev = P.op("scalar", lambda e: e.activation(
            out=p8T[:], in_=ps_lg[:], func=AF.Exp),
            waits=[(MM, mm), guard("p8T")], inc=EV)
        setg("ps_lg", (EV, ev))
        p8_ev = ev

        def tree_and_bcast(src_t, w0):
            mmv = None
            for c2 in range(2):
                o = c2 * 512
                xl = s_tre.dma("sync", lambda e, src_t=src_t, o=o: e.dma_start(
                    out=g_sh[0:4, :], in_=src_t[4:8, o:o + 512]),
                    waits=w0 + [guard("g_sh")])
                d = P.op("vector", lambda e, src_t=src_t, o=o: e.tensor_max(
                    g_m1[0:4, :], src_t[0:4, o:o + 512], g_sh[0:4, :]),
                    waits=[xl, guard("g_m1")], inc=DV)
                xl = s_tre.dma("sync", lambda e: e.dma_start(
                    out=g_sh[0:2, :], in_=g_m1[2:4, :]), waits=[(DV, d)])
                d = P.op("vector", lambda e: e.tensor_max(
                    g_m1[0:2, :], g_m1[0:2, :], g_sh[0:2, :]),
                    waits=[xl], inc=DV)
                xl = s_tre.dma("sync", lambda e: e.dma_start(
                    out=g_sh[0:1, :], in_=g_m1[1:2, :]), waits=[(DV, d)])
                d = P.op("vector", lambda e: e.tensor_max(
                    g_m1[0:1, :], g_m1[0:1, :], g_sh[0:1, :]),
                    waits=[xl], inc=DV)
                setg("g_sh", (DV, d))
                mmv = P.op("tensor", lambda e, o=o: e.matmul(
                    ps_b8[:, o:o + 512], c_ones18[0:1, :], g_m1[0:1, :],
                    start=True, stop=True),
                    waits=[(DV, d), s_cst.ok()] + (guard("ps_b8")
                                                   if c2 == 0 else []),
                    inc=MM)
                setg("g_m1", (MM, mmv))
            return mmv

        mm = tree_and_bcast(p8T, [(EV, p8_ev)])
        dv = P.op("vector", lambda e: e.tensor_tensor(
            g_keep[:], p8T[:], ps_b8[:], op=ALU.is_ge),
            waits=[(MM, mm), guard("g_keep")], inc=DV)
        dv = P.op("vector", lambda e: e.scalar_tensor_tensor(
            out=g_keep[:], in0=g_keep[:], scalar=-1e30, in1=p8T[:],
            op0=ALU.mult, op1=ALU.add), waits=[(DV, dv)], inc=DV)
        mm = tree_and_bcast(g_keep, [(DV, dv)])
        dv = P.op("vector", lambda e: e.tensor_tensor(
            g_keep[:], p8T[:], ps_b8[:], op=ALU.is_ge),
            waits=[(MM, mm)], inc=DV)
        mm = None
        for tc2 in range(2):
            mm = P.op("tensor", lambda e, tc2=tc2: e.matmul(
                ps_s1g[0:1, tc2 * 512:(tc2 + 1) * 512], c_ones81[:],
                p8T[:, tc2 * 512:(tc2 + 1) * 512], start=True, stop=True),
                waits=([(DV, dv), guard("ps_s1g")] if tc2 == 0 else []),
                inc=(MM if tc2 == 1 else None))
        rmm = None
        for tc2 in range(2):
            d2 = P.op("vector", lambda e, tc2=tc2: e.reciprocal(
                out=g_m1[0:1, :], in_=ps_s1g[0:1, tc2 * 512:(tc2 + 1) * 512]),
                waits=[(MM, mm), guard("g_m1")], inc=DV)
            setg("ps_s1g", (DV, d2))
            rmm = P.op("tensor", lambda e, tc2=tc2: e.matmul(
                ps_b8[:, tc2 * 512:(tc2 + 1) * 512], c_ones18[0:1, :],
                g_m1[0:1, :], start=True, stop=True),
                waits=[(DV, d2)], inc=MM)
            setg("g_m1", (MM, rmm))
        dv = P.op("vector", lambda e: e.tensor_mul(
            g_keep[:], g_keep[:], p8T[:]), waits=[(MM, rmm), (DV, dv)], inc=DV)
        setg("p8T", (DV, dv))
        dv = P.op("vector", lambda e: e.tensor_tensor(
            p8nT[:], g_keep[:], ps_b8[:], op=ALU.mult),
            waits=[(DV, dv), guard("p8nT")], inc=DV)
        setg("ps_b8", (DV, dv))
        setg("g_keep", (DV, dv))
        p8n_dv = dv
        if debug:
            s_dbg.dma("sync", lambda e, ht0=ht0: e.dma_start(
                out=dbg["comb"][:, ht0:ht0 + THALF], in_=p8nT[:]),
                waits=[(DV, dv)])

        dwn_ok = None
        for xe in range(b_experts):
            shared = xe >= 8

            def load_pair(fp, xe=xe, shared=shared):
                gb = gub[(fp % 2) * 2]
                ub = gub[(fp % 2) * 2 + 1]
                if not shared:
                    g_src = guwT[xe, :, fp * 128:(fp + 1) * 128]
                    u_src = guwT[xe, :, F + fp * 128: F + (fp + 1) * 128]
                else:
                    sh = xe - 8
                    g_src = sguT[:, sh * F + fp * 128: sh * F + (fp + 1) * 128]
                    u_src = sguT[:, 2816 + sh * F + fp * 128:
                                 2816 + sh * F + (fp + 1) * 128]
                i_g = (fp % 2) * 2
                i_u = (fp % 2) * 2 + 1
                ok_g = s_gu[i_g].dma("gpsimd", lambda e, gb=gb, g_src=g_src:
                                     e.dma_start(
                    out=gb[:].rearrange("p (kc f) -> p kc f", kc=16),
                    in_=g_src.rearrange("(kc p) f -> p kc f", p=128)),
                    waits=[guard(("gub", i_g))])
                ok_u = s_gu[i_u].dma("gpsimd", lambda e, ub=ub, u_src=u_src:
                                     e.dma_start(
                    out=ub[:].rearrange("p (kc f) -> p kc f", kc=16),
                    in_=u_src.rearrange("(kc p) f -> p kc f", p=128)),
                    waits=[guard(("gub", i_u))])
                return (ok_g, ok_u)

            pair_wl = {}
            pair_wl[0] = load_pair(0)
            if NPAIR > 1:
                pair_wl[1] = load_pair(1)
            for fp in range(NPAIR):
                if not shared:
                    src = downT[xe, fp * 128:(fp + 1) * 128, :]
                else:
                    sh = xe - 8
                    src = sdT[sh * F + fp * 128: sh * F + (fp + 1) * 128, :]
                dwn_ok = s_dnl.dma("gpsimd", lambda e, src=src, fp=fp:
                                   e.dma_start(
                    out=dwn_sb[:, fp * 2048:(fp + 1) * 2048], in_=src),
                    waits=([guard("dwn")] if fp == 0 else []))

            if not shared:
                cmm = None
                for tc2 in range(2):
                    cmm = P.op("tensor", lambda e, tc2=tc2, xe=xe: e.matmul(
                        ps_g[:, tc2 * 512:(tc2 + 1) * 512],
                        c_sel[:, xe * 128:(xe + 1) * 128],
                        p8nT[:, tc2 * 512:(tc2 + 1) * 512],
                        start=True, stop=True),
                        waits=([(DV, p8n_dv), (EV, xn2T_ev), s_cst.ok(),
                                guard("ps_gW")] if tc2 == 0 else []),
                        inc=(MM if tc2 == 1 else None))
                cdv = P.op("vector", lambda e: e.tensor_copy(
                    out=cb_sb[:], in_=ps_g[:]),
                    waits=[(MM, cmm), guard("cb_sb")], inc=DV)
                setg("ps_gW", (DV, cdv))

            for fp in range(NPAIR):
                gb = gub[(fp % 2) * 2]
                ub = gub[(fp % 2) * 2 + 1]
                for tc2 in range(2):
                    for kc in range(16):
                        mm = P.op("tensor", lambda e, kc=kc, tc2=tc2, gb=gb:
                                  e.matmul(
                            ps_g[:, tc2 * 512:(tc2 + 1) * 512],
                            gb[:, kc * 128:(kc + 1) * 128],
                            xn2T[:, kc * THALF + tc2 * 512:
                                 kc * THALF + (tc2 + 1) * 512],
                            start=(kc == 0), stop=(kc == 15)),
                            waits=([pair_wl[fp][0], (EV, xn2T_ev),
                                    (DV, p8n_dv), guard("ps_gW")]
                                   if (kc == 0 and tc2 == 0) else []),
                            inc=(MM if (kc == 15 and tc2 == 1) else None))
                g_mm = mm
                for tc2 in range(2):
                    for kc in range(16):
                        mm = P.op("tensor", lambda e, kc=kc, tc2=tc2, ub=ub:
                                  e.matmul(
                            ps_u[:, tc2 * 512:(tc2 + 1) * 512],
                            ub[:, kc * 128:(kc + 1) * 128],
                            xn2T[:, kc * THALF + tc2 * 512:
                                 kc * THALF + (tc2 + 1) * 512],
                            start=(kc == 0), stop=(kc == 15)),
                            waits=([pair_wl[fp][1], guard("ps_uW")]
                                   if (kc == 0 and tc2 == 0) else []),
                            inc=(MM if (kc == 15 and tc2 == 1) else None))
                u_mm = mm
                ev = P.op("scalar", lambda e: e.activation(
                    out=sg_f[:], in_=ps_g[:], func=AF.Silu),
                    waits=[(MM, g_mm), guard("sg_f")], inc=EV)
                setg("ps_gW", (EV, ev))
                dv = P.op("vector", lambda e, fp=fp: e.tensor_tensor(
                    act_h[:, fp * 1024:(fp + 1) * 1024], sg_f[:], ps_u[:],
                    op=ALU.mult), waits=[(EV, ev), (MM, u_mm)], inc=DV)
                setg("sg_f", (DV, dv))
                setg("ps_uW", (DV, dv))
                if not shared:
                    dv = P.op("vector", lambda e, fp=fp: e.tensor_tensor(
                        act_h[:, fp * 1024:(fp + 1) * 1024],
                        act_h[:, fp * 1024:(fp + 1) * 1024], cb_sb[:],
                        op=ALU.mult), waits=[(DV, dv)], inc=DV)
                setg(("gub", (fp % 2) * 2), (MM, g_mm))
                setg(("gub", (fp % 2) * 2 + 1), (MM, u_mm))
                if fp + 2 < NPAIR:
                    pair_wl[fp + 2] = load_pair(fp + 2)
            act_dv = P.cnt(DV)
            setg("cb_sb", (DV, act_dv))

            for ts8 in range(8):
                for fp in range(NPAIR):
                    for hc in range(4):
                        mm = P.op("tensor",
                                  lambda e, fp=fp, hc=hc, ts8=ts8: e.matmul(
                            ps_y[:, hc * 512:(hc + 1) * 512],
                            act_h[:, fp * 1024 + ts8 * 128:
                                  fp * 1024 + (ts8 + 1) * 128],
                            dwn_sb[:, fp * 2048 + hc * 512:
                                   fp * 2048 + (hc + 1) * 512],
                            start=(fp == 0), stop=(fp == NPAIR - 1)),
                            waits=([(DV, act_dv), dwn_ok, guard("ps_y")]
                                   if (fp == 0 and hc == 0) else []),
                            inc=(MM if (fp == NPAIR - 1 and hc == 3)
                                 else None))
                setg("dwn", (MM, mm))
                sc = ysc[ts8 % 2]
                ev = P.op("scalar", lambda e, sc=sc: e.copy(
                    out=sc[:], in_=ps_y[:]),
                    waits=[(MM, mm), guard(("ysc", ts8 % 2))], inc=EV)
                setg("ps_y", (EV, ev))
                if xe == 0:
                    ya = s_ya[ts8].dma("gpsimd", lambda e, sc=sc, ts8=ts8:
                                       e.dma_start(
                        out=y_half[:, ts8 * 2048:(ts8 + 1) * 2048],
                        in_=sc[:]),
                        waits=[(EV, ev), guard(("yseg", ts8))])
                else:
                    ya = s_ya[ts8].dma("gpsimd", lambda e, sc=sc, ts8=ts8:
                                       e.dma_start(
                        out=y_half[:, ts8 * 2048:(ts8 + 1) * 2048],
                        in_=sc[:], accum_op=ALU.add),
                        waits=[(EV, ev), guard(("yseg", ts8))])
                setg(("yseg", ts8), ya)
                setg(("ysc", ts8 % 2), ya)

        sdv = sev = None
        for tc in range(8):
            wts = [guard(("yseg", tc))]
            sdv = P.op("vector", lambda e, tc=tc: e.reduce_sum(
                out=lnst[:, tc:tc + 1],
                in_=y_half[:, tc * 2048:(tc + 1) * 2048], axis=AX.X),
                waits=wts + [guard("lnst")], inc=DV)
            sev = P.op("scalar", lambda e, tc=tc: e.activation(
                out=dum_b[:], in_=y_half[:, tc * 2048:(tc + 1) * 2048],
                func=AF.Square, accum_out=lnst[:, 8 + tc:9 + tc]),
                waits=wts + [guard(("ysc", 0))], inc=EV)
            setg(("ysc", 0), (EV, sev))
        dv = P.op("vector", lambda e: e.tensor_scalar(
            out=lnst[:, 16:24], in0=lnst[:, 0:8], scalar1=1.0 / 2048.0,
            scalar2=None, op0=ALU.mult),
            waits=[(DV, sdv), (EV, sev)], inc=DV)
        dv = P.op("vector", lambda e: e.scalar_tensor_tensor(
            out=sq2[:, 0:8], in0=lnst[:, 16:24], scalar=-1.0,
            in1=lnst[:, 16:24], op0=ALU.mult, op1=ALU.mult),
            waits=[(DV, dv), guard("sq2")], inc=DV)
        dv = P.op("vector", lambda e: e.scalar_tensor_tensor(
            out=sq2[:, 8:16], in0=lnst[:, 8:16], scalar=1.0 / 2048.0,
            in1=sq2[:, 0:8], op0=ALU.mult, op1=ALU.add),
            waits=[(DV, dv)], inc=DV)
        ev = P.op("scalar", lambda e: e.activation(
            out=sq2[:, 16:24], in_=sq2[:, 8:16], func=AF.Ln,
            bias=c_eps[:, 0:1]), waits=[(DV, dv)], inc=EV)
        ev = P.op("scalar", lambda e: e.activation(
            out=lnst[:, 24:32], in_=sq2[:, 16:24], func=AF.Exp, scale=-0.5),
            waits=[(EV, ev)], inc=EV)
        setg("sq2", (EV, ev))
        lastdv = None
        for tc in range(8):
            tok0 = ht0 + tc * 128
            buf = x1b[tc % 2]
            xl = s_xb[tc % 2].dma("sync", lambda e, buf=buf, tok0=tok0:
                                  e.dma_start(
                out=buf[:], in_=x1_hbm[tok0:tok0 + 128, :]),
                waits=[guard(("x1b", tc % 2))])
            dv = P.op("vector", lambda e, tc=tc: e.tensor_scalar(
                out=o_f[:], in0=y_half[:, tc * 2048:(tc + 1) * 2048],
                scalar1=lnst[:, 16 + tc:17 + tc],
                scalar2=lnst[:, 24 + tc:25 + tc],
                op0=ALU.subtract, op1=ALU.mult),
                waits=[(EV, ev), guard("o_f")], inc=DV)
            dv = P.op("vector", lambda e, buf=buf: e.tensor_add(
                o_f[:], o_f[:], buf[:]), waits=[(DV, dv), xl], inc=DV)
            setg(("x1b", tc % 2), (DV, dv))
            st = s_ost.dma("sync", lambda e, tok0=tok0: e.dma_start(
                out=out_hbm[tok0:tok0 + 128, :], in_=o_f[:]),
                waits=[(DV, dv)])
            setg("o_f", st)
            lastdv = dv
        setg("lnst", (DV, P.cnt(DV)), (EV, P.cnt(EV)))
        for t in range(8):
            setg(("yseg", t), (DV, P.cnt(DV)), (EV, P.cnt(EV)))
        setg("p8nT", (MM, P.cnt(MM)))

    if debug and debug >= 3 and do_b:
        wend2 = [(MM, P.cnt(MM)), (EV, P.cnt(EV)), (DV, P.cnt(DV))] +                 [s_ya[t].ok() for t in range(8)]
        s_dbg.dma("sync", lambda e: e.dma_start(out=dbg["dbg_xn2"],
                                                in_=xn2T[:]), waits=wend2)
        s_dbg.dma("sync", lambda e: e.dma_start(out=dbg["dbg_y"],
                                                in_=y_half[:]), waits=wend2)
        s_dbg.dma("sync", lambda e: e.dma_start(out=dbg["dbg_act"],
                                                in_=act_h[:]), waits=wend2)
        s_dbg.dma("sync", lambda e: e.dma_start(out=dbg["dbg_cb"],
                                                in_=cb_sb[:]), waits=wend2)

    final = [(s_, c) for s_, c in P.counts.items() if c > 0]
    P.op("sync", lambda e: e.nop(), waits=final)
    P.op("gpsimd", lambda e: e.nop(), waits=final)

    _emit(nc, P, stack)
    b_stack.close()
    stack.close()
    nc.finalize()
    _NC_CACHE[key] = nc
    return nc



def _rope_perm(nheads):
    return np.concatenate([h * 128 + np.r_[np.arange(0, 128, 2),
                                           np.arange(1, 128, 2)]
                           for h in range(nheads)])


def _prep(inputs):
    f32 = np.float32
    X = np.ascontiguousarray(np.asarray(inputs["L0_attn_x"], dtype=f32)
                             .reshape(BATCH * SEQ, D))
    wq = np.asarray(inputs["L0_attn_wq"], dtype=f32)
    wk = np.asarray(inputs["L0_attn_wk"], dtype=f32)
    wv = np.asarray(inputs["L0_attn_wv"], dtype=f32)
    wo = np.asarray(inputs["L0_attn_wo"], dtype=f32)
    gw = np.asarray(inputs["L0_ffn_gate_weight"], dtype=f32)
    guw = np.asarray(inputs["L0_ffn_gate_up_weights"], dtype=f32)
    dw = np.asarray(inputs["L0_ffn_down_weights"], dtype=f32)
    sgu = np.asarray(inputs["L0_ffn_shared_gate_up"], dtype=f32)
    sd = np.asarray(inputs["L0_ffn_shared_down"], dtype=f32)

    pq = _rope_perm(HQ)
    pk = _rope_perm(HKV)
    shared = {
        "wqT": np.ascontiguousarray(wq[pq].T),
        "wkT": np.ascontiguousarray(wk[pk].T),
        "wvT": np.ascontiguousarray(wv.T),
        "woT": np.ascontiguousarray(wo.T),
        "gwT": np.ascontiguousarray(gw.T),
        "guwT": np.ascontiguousarray(guw.transpose(0, 2, 1)),
        "downT": np.ascontiguousarray(dw.transpose(0, 2, 1)),
        "sguT": np.ascontiguousarray(sgu.T),
        "sdT": np.ascontiguousarray(sd.T),
    }
    jj = np.arange(128)[:, None]
    aa = np.arange(128)[None, :]
    shared["tri_lo"] = np.where(aa <= jj, 0.0, NEG).astype(f32)
    shared["tri_hi"] = np.where(jj <= aa, 0.0, NEG).astype(f32)
    shared["ident_f"] = np.eye(128, dtype=f32)
    shared["ones81"] = np.ones((8, 1), f32)
    shared["ones18"] = np.ones((1, 8), f32)
    sel = np.zeros((8, 8 * 128), dtype=ml_dtypes.bfloat16)
    for e in range(8):
        sel[e, e * 128:(e + 1) * 128] = 1.0
    shared["sel_e"] = sel

    inv = 1.0 / 10000.0 ** (np.arange(0, HD, 2, dtype=np.float64) / HD)
    in_maps = []
    for c in range(8):
        r0 = c * TCORE
        h = c % 2
        m = dict(shared)
        m["x_own"] = X[r0:r0 + TCORE]
        m["x_halo"] = (np.ascontiguousarray(X[r0 - W:r0]) if h == 1
                       else np.zeros((W, D), f32))
        pos = (h * TCORE - W) + np.arange(2560)
        ang = pos[None, :] * inv[:, None]
        cs = np.cos(ang)
        sn = np.sin(ang)
        m["cos_t"] = np.ascontiguousarray(
            np.concatenate([cs, cs], 0)).astype(ml_dtypes.bfloat16)
        m["sin_t"] = np.ascontiguousarray(
            np.concatenate([-sn, sn], 0)).astype(ml_dtypes.bfloat16)
        m["halo_b"] = np.full((128, 1), NEG if h == 0 else 0.0, f32)
        in_maps.append(m)
    return in_maps


def kernel(**inputs):
    in_maps = _prep(inputs)
    nc = build_program(debug=False)
    res = run_bass_kernel_spmd(nc, in_maps, core_ids=list(range(8)))
    out = np.concatenate([np.asarray(r["out"], dtype=np.float32)
                          for r in res.results], axis=0)
    return out.reshape(BATCH, SEQ, D)
